# version 08
# speedup vs baseline: 1.6950x; 1.2231x over previous
import numpy as np
from collections import defaultdict
from contextlib import ExitStack

import concourse.bass as bass
import concourse.mybir as mybir
from concourse.bass_utils import run_bass_kernel_spmd

F32 = mybir.dt.float32
BF16 = mybir.dt.bfloat16
I16 = mybir.dt.int16
U16 = mybir.dt.uint16
I32 = mybir.dt.int32
AF = mybir.ActivationFunctionType
ALU = mybir.AluOpType
AX = mybir.AxisListType

CORES = 8
T = 2048
D = 1024
NSEQ = 2
NTOK = NSEQ * T
H = 8
CH = 128
NCH = T // CH
GC = 2
INTER_A = 4096
INTER_F = 3072
EPS = 1e-5
MAGIC = 12582912.0
QMAX = 32767.0

ENGINES = ("pe", "dve", "act", "pool", "sp")
SECTION = {"pe": "tensor", "dve": "vector", "act": "scalar", "pool": "gpsimd",
           "sp": "sync"}


class FW:

    LIMIT = None

    def __init__(self, nc, es):
        self.nc = nc
        self.es = es
        self.nops = 0
        self.eng = {"pe": nc.tensor, "dve": nc.vector, "act": nc.scalar,
                    "pool": nc.gpsimd, "sp": nc.sync}
        self.csem = {e: es.enter_context(nc.semaphore("c_" + e)) for e in ENGINES}
        self.ccnt = {e: 0 for e in ENGINES}
        self.dsem = {}
        self.dcnt = defaultdict(int)
        self.mem = {}
        self.queues = {e: [] for e in ENGINES}
        self.waited = {e: defaultdict(int) for e in ENGINES}

    def _sem_of(self, tok):
        kind, key, val = tok
        return (self.csem[key] if kind == "c" else self.dsem[key]), val

    def _need(self, cons, tok, waits):
        if tok is None:
            return
        sem, val = self._sem_of(tok)
        k = id(sem)
        if self.waited[cons][k] >= val:
            return
        cur = waits.get(k)
        if cur is None or cur[1] < val:
            waits[k] = (sem, val)

    def _dep_waits(self, eng, reads, writes, accum):
        waits = {}
        for k in reads:
            st = self.mem.get(k)
            if st is not None:
                self._need(eng, st[0], waits)
        for k in writes:
            st = self.mem.get(k)
            if st is not None:
                w, readers, was_accum = st
                if not (accum and was_accum and w is not None and w[0] == "c"
                        and w[1] == "pe"):
                    self._need(eng, w, waits)
                for r in readers:
                    self._need(eng, r, waits)
        for _, (sem, val) in waits.items():
            self.waited[eng][id(sem)] = max(self.waited[eng][id(sem)], val)
        return list(waits.values())

    def _update_mem(self, tok, reads, writes, pe_accum):
        for k in reads:
            st = self.mem.setdefault(k, [None, [], False])
            st[1].append(tok)
        for k in writes:
            self.mem[k] = [tok, [], pe_accum]

    def op(self, eng, fn, reads=(), writes=(), accum=False):
        self.nops += 1
        if (FW.LIMIT is not None and self.nops > FW.LIMIT
                and not getattr(self, "exempt", False)):
            return None
        waits = self._dep_waits(eng, reads, writes, accum)
        self.ccnt[eng] += 1
        tok = ("c", eng, self.ccnt[eng])
        self.queues[eng].append((waits, fn, self.csem[eng], 1))
        self._update_mem(tok, reads, writes, accum and eng == "pe")
        return tok

    def dma(self, queue, fn, reads=(), writes=(), sem_key=None):
        self.nops += 1
        if (FW.LIMIT is not None and self.nops > FW.LIMIT
                and not getattr(self, "exempt", False)):
            return None
        if sem_key is None:
            sem_key = writes[0] if writes else (reads[0] if reads else "misc")
        if sem_key not in self.dsem:
            self.dsem[sem_key] = self.es.enter_context(
                self.nc.semaphore("d_%s" % str(sem_key)))
        waits = self._dep_waits(queue, reads, writes, False)
        self.dcnt[sem_key] += 16
        tok = ("d", sem_key, self.dcnt[sem_key])
        self.queues[queue].append((waits, fn, self.dsem[sem_key], 16))
        self._update_mem(tok, reads, writes, False)
        return tok

    def fence(self):
        toks = [("c", e, self.ccnt[e]) for e in ENGINES if self.ccnt[e] > 0]
        toks += [("d", k, self.dcnt[k]) for k in self.dcnt]
        for e in ENGINES:
            waits = {}
            for t in toks:
                if t[0] == "c" and t[1] == e:
                    continue
                self._need(e, t, waits)
            for _, (sem, val) in waits.items():
                self.waited[e][id(sem)] = max(self.waited[e][id(sem)], val)
            if waits:
                self.queues[e].append((list(waits.values()), None, None, 0))

    def alias_writer(self, keys, tok):
        for k in keys:
            st = self.mem.get(k)
            if st is not None:
                st[0] = tok

    def barrier_wait(self, eng, toks):
        waits = {}
        for t in toks:
            self._need(eng, t, waits)
        for _, (sem, val) in waits.items():
            self.waited[eng][id(sem)] = max(self.waited[eng][id(sem)], val)
        if waits:
            self.queues[eng].append((list(waits.values()), None, None, 0))

    def finalize(self, block):
        for e in ENGINES:
            q = self.queues[e]
            if not q:
                continue

            def make_body(q=q):
                def body(h):
                    for waits, fn, sem, amt in q:
                        for s, v in waits:
                            h.wait_ge(s, v)
                        if fn is not None:
                            fn(h).then_inc(sem, amt)
                return body
            getattr(block, SECTION[e])(make_body())

    def stats(self):
        return {e: len(self.queues[e]) for e in ENGINES}



def emit_consts(fw, c):
    for k in ("ident_f", "ident_b"):
        if k in c:
            fw.op("pool", lambda h, k=k: h.memset(c[k][:], 1.0), writes=[k])
            fw.op("pool", lambda h, k=k: h.affine_select(c[k][:], c[k][:],
                                                         [[1, 128]], ALU.is_equal,
                                                         0.0, base=0,
                                                         channel_multiplier=-1),
                  reads=[k], writes=[k])
    if "mask" in c:
        fw.op("pool", lambda h: h.memset(c["mask"][:], 1.0), writes=["mask"])
        fw.op("pool", lambda h: h.affine_select(c["mask"][:], c["mask"][:],
                                                [[1, 128]], ALU.is_ge, 0.0,
                                                base=0, channel_multiplier=-1),
              reads=["mask"], writes=["mask"])
    for k in ("ones_c", "ones_r", "ones_cb"):
        if k in c:
            fw.op("pool", lambda h, k=k: h.memset(c[k][:], 1.0), writes=[k])


def load_pp_vec(fw, nc, dst, dram_ap, n, key):
    fw.dma("sp", lambda h: h.dma_start(out=dst, in_=dram_ap), writes=[key],
           sem_key=key)


def cross_partition_max(fw, c, src_col, tmp128, rowbuf, psum, out_row, keys):
    k_tmp, k_ps, k_row, k_out = keys[:4]
    fw.op("dve", lambda h: h.memset(tmp128[:], 0.0), writes=[k_tmp])
    fw.op("dve", lambda h: h.tensor_copy(tmp128[:, 0:1], src_col),
          reads=[keys[4]], writes=[k_tmp])
    fw.op("pe", lambda h: h.transpose(psum[:], tmp128[:], c["ident_f"][:]),
          reads=[k_tmp, "ident_f"], writes=[k_ps])
    fw.op("dve", lambda h: h.tensor_copy(rowbuf[:], psum[0:1, :]),
          reads=[k_ps], writes=[k_row])
    fw.op("dve", lambda h: h.reduce_max(out_row, rowbuf[:], axis=AX.X),
          reads=[k_row], writes=[k_out])



def build_l1(phases=("init", "A", "qkvo", "scan", "mlp"), nscan=None):
    nc = bass.Bass("TRN2", target_bir_lowering=False)
    x_d = nc.dram_tensor("x", [NTOK, D], F32, kind="ExternalInput").ap()
    wq_d = nc.dram_tensor("wq", [D, D], F32, kind="ExternalInput").ap()
    wk_d = nc.dram_tensor("wk", [D, D], F32, kind="ExternalInput").ap()
    wv_d = nc.dram_tensor("wv", [D, D], F32, kind="ExternalInput").ap()
    wo_d = nc.dram_tensor("wo", [D, D], F32, kind="ExternalInput").ap()
    wg_d = nc.dram_tensor("wgate", [D, INTER_A], F32, kind="ExternalInput").ap()
    wu_d = nc.dram_tensor("wup", [D, INTER_A], F32, kind="ExternalInput").ap()
    wd_d = nc.dram_tensor("wdown", [INTER_A, D], F32, kind="ExternalInput").ap()
    anw_d = nc.dram_tensor("attn_norm_w", [D], F32, kind="ExternalInput").ap()
    onw_d = nc.dram_tensor("attn_out_norm_w", [128], F32, kind="ExternalInput").ap()
    mnw_d = nc.dram_tensor("mlp_norm_w", [D], F32, kind="ExternalInput").ap()
    s_out = nc.dram_tensor("s_f", [D, NTOK], F32, kind="ExternalOutput").ap()
    mx_out = nc.dram_tensor("m_x", [1, 1], F32, kind="ExternalOutput").ap()
    wgb_d = nc.dram_tensor("wgate_bf", [D, INTER_A], BF16).ap()
    wub_d = nc.dram_tensor("wup_bf", [D, INTER_A], BF16).ap()
    wdb_d = nc.dram_tensor("wdown_bf", [INTER_A, D], BF16).ap()
    wvb_d = nc.dram_tensor("wv_bf", [D, D], BF16).ap()

    GC4 = 4
    W = 512

    es = ExitStack()
    sb = lambda n, s, dt=F32: es.enter_context(nc.sbuf_tensor(n, s, dt))
    ps = lambda n, s, dt=F32: es.enter_context(nc.psum_tensor(n, s, dt))
    with es:
        xn_f = sb("xn_f", [128, 8 * T])
        AB = sb("AB", [128, 53504], BF16)
        AFa = sb("AFa", [128, 7168])
        c = {"ident_f": sb("ident_f", [128, 128]),
             "ident_b": sb("ident_b", [128, 128], BF16),
             "mask": sb("mask", [128, 128]),
             "ones_c": sb("ones_c", [128, 1]),
             "ones_r": sb("ones_r", [1, 128]),
             "ones_cb": sb("ones_cb", [128, 1], BF16)}
        wn_a = sb("wn_a", [128, 8])
        wn_m = sb("wn_m", [128, 8])
        wn_o = sb("wn_o", [128, 1])
        row1 = sb("row1", [1, 512])
        row2 = sb("row2", [1, 512])
        rowb = sb("rowb", [1, 128])
        mrow = sb("mrow", [1, 1])
        mx_part = sb("mx_part", [128, 1])
        stats = sb("stats", [128, 8])
        cst = sb("cst", [1, 4])
        cpp = sb("cpp", [128, 1])
        pb = [ps("pb%d" % i, [128, 512]) for i in range(7)]
        pT = ps("pT", [128, 1024], BF16)
        BK = ["bank%d" % i for i in range(7)]
        fw = FW(nc, es)
        block = es.enter_context(nc.Block())

        emit_consts(fw, c)
        fw.op("dve", lambda h: h.memset(mx_part[:], 0.0), writes=["mx_part"])
        fw.op("dve", lambda h: h.memset(cst[0:1, 0:1], EPS), writes=["cst"])
        fw.op("dve", lambda h: h.memset(cst[0:1, 1:2], 1.0 / D), reads=["cst"], writes=["cst"])
        fw.op("dve", lambda h: h.memset(cst[0:1, 2:3], 1.0 / 128), reads=["cst"], writes=["cst"])
        fw.op("dve", lambda h: h.memset(cst[0:1, 3:4], -0.5), reads=["cst"], writes=["cst"])
        fw.op("dve", lambda h: h.memset(cpp[:, 0:1], EPS), writes=["cpp"])

        for dc in range(8):
            fw.dma("sp", lambda h, dc=dc: h.dma_start(
                out=wn_a[:, dc:dc + 1], in_=anw_d[dc * 128:(dc + 1) * 128][:, None]),
                writes=[("wn_a", dc)], sem_key="wna%d" % dc)
            fw.dma("sp", lambda h, dc=dc: h.dma_start(
                out=wn_m[:, dc:dc + 1], in_=mnw_d[dc * 128:(dc + 1) * 128][:, None]),
                writes=[("wn_m", dc)], sem_key="wnm%d" % dc)
        fw.dma("sp", lambda h: h.dma_start(out=wn_o[:], in_=onw_d[:, None]),
               writes=["wn_o"], sem_key="wno")

        stage_f = AFa[:, 0:4096]
        stage_b = AB[:, 0:4096]
        for name, src, dst, rows, cols in ((
                ("wg", wg_d, wgb_d, D, INTER_A), ("wu", wu_d, wub_d, D, INTER_A),
                ("wd", wd_d, wdb_d, INTER_A, D), ("wv", wv_d, wvb_d, D, D))
                if "init" in phases else ()):
            for r in range(rows // 128):
                fw.dma("sp", lambda h, src=src, r=r, cols=cols: h.dma_start(
                    out=stage_f[:, 0:cols], in_=src[r * 128:(r + 1) * 128, :]),
                    writes=["stage_f"], sem_key="wcast")
                fw.op("dve", lambda h, cols=cols: h.tensor_copy(
                    stage_b[:, 0:cols], stage_f[:, 0:cols]),
                    reads=["stage_f"], writes=["stage_b"])
                fw.dma("sp", lambda h, dst=dst, r=r, cols=cols: h.dma_start(
                    out=dst[r * 128:(r + 1) * 128, :], in_=stage_b[:, 0:cols]),
                    reads=["stage_b"], writes=[(name + "_bf", r)], sem_key="wcast_o")
        fw.fence()

        def ab(start, ln):
            return AB[:, start:start + ln]

        def af(start, ln):
            return AFa[:, start:start + ln]

        wq_sb, wk_sb = ab(0, 8192), ab(8192, 8192)
        wo_sb = ab(16384, 8192)
        h_bf = ab(24576, 4096)
        q_bf, k_bf = ab(28672, 4096), ab(32768, 4096)
        ktm, vtm = ab(36864, 4096), ab(40960, 4096)
        S_bf = ab(45056, 1024)
        aT_bf = ab(46080, 256)
        onw_bf = ab(46336, 4096)
        vfm_bf = ab(50432, 512)
        wvsl = [ab(50944, 1024), ab(51968, 1024)]
        dump_bf = ab(52992, 512)
        x_tm = af(0, 1024)
        dump = af(1024, 1024)
        S_sb = af(2048, 1024)
        o2_sb = af(3072, 4096)
        h2_bf = ab(0, 4096)
        c_bf = ab(4096, 16384)
        wgsl = [ab(20480, 4096), ab(24576, 4096)]
        wusl = [ab(28672, 4096), ab(32768, 4096)]
        wdsl = [ab(36864, 4096), ab(40960, 4096)]
        sil_f = af(2048, 512)
        g_f = af(2560, 512)
        sg_f = af(3072, 4096)

        for seq in range(NSEQ):
            for g in range(T // 128 if "A" in phases else 0):
                t0 = g * 128
                fw.dma("sp", lambda h, seq=seq, t0=t0: h.dma_start(
                    out=x_tm[:, :], in_=x_d[seq * T + t0:seq * T + t0 + 128, :]),
                    writes=["x_tm"], sem_key="x_in")
                fw.op("dve", lambda h: h.reduce_sum(stats[:, 0:1], x_tm[:, :], axis=AX.X),
                      reads=["x_tm"], writes=[("st", 0)])
                fw.op("dve", lambda h: h.tensor_scalar(stats[:, 1:2], stats[:, 0:1],
                                                       1.0 / D, None, ALU.mult),
                      reads=[("st", 0)], writes=[("st", 1)])
                fw.op("act", lambda h: h.activation(dump[:, :], x_tm[:, :], AF.Square,
                                                    accum_out=stats[:, 2:3]),
                      reads=["x_tm"], writes=["dump", ("st", 2)])
                fw.op("dve", lambda h: h.tensor_tensor(stats[:, 3:4], stats[:, 1:2],
                                                       stats[:, 1:2], op=ALU.mult),
                      reads=[("st", 1)], writes=[("st", 3)])
                fw.op("dve", lambda h: h.tensor_scalar(stats[:, 4:5], stats[:, 2:3],
                                                       1.0 / D, None, ALU.mult),
                      reads=[("st", 2)], writes=[("st", 4)])
                fw.op("dve", lambda h: h.tensor_tensor(stats[:, 5:6], stats[:, 4:5],
                                                       stats[:, 3:4], op=ALU.subtract),
                      reads=[("st", 3), ("st", 4)], writes=[("st", 5)])
                fw.op("act", lambda h: h.activation(stats[:, 6:7], stats[:, 5:6],
                                                    AF.Sqrt, bias=cpp[:, 0:1]),
                      reads=[("st", 5), "cpp"], writes=[("st", 6)])
                fw.op("dve", lambda h: h.reciprocal(stats[:, 7:8], stats[:, 6:7]),
                      reads=[("st", 6)], writes=[("st", 7)])
                fw.op("dve", lambda h: h.tensor_scalar(x_tm[:, :], x_tm[:, :],
                                                       stats[:, 1:2], stats[:, 7:8],
                                                       ALU.subtract, ALU.mult),
                      reads=["x_tm", ("st", 1), ("st", 7)], writes=["x_tm"])
                fw.op("dve", lambda h: h.reduce_max(stats[:, 0:1], x_tm[:, :], axis=AX.X,
                                                    apply_absolute_value=True),
                      reads=["x_tm"], writes=[("st", 0)])
                fw.op("dve", lambda h: h.tensor_tensor(mx_part[:], mx_part[:],
                                                       stats[:, 0:1], op=ALU.max),
                      reads=[("st", 0), "mx_part"], writes=["mx_part"])
                for dc in range(8):
                    b = dc % 2
                    fw.op("pe", lambda h, dc=dc, b=b: h.transpose(
                        pb[b][:, 0:128], x_tm[:, dc * 128:(dc + 1) * 128], c["ident_f"][:]),
                        reads=["x_tm", "ident_f"], writes=[BK[b]])
                    fw.op("dve", lambda h, dc=dc, t0=t0, b=b: h.tensor_copy(
                        xn_f[:, dc * T + t0:dc * T + t0 + 128], pb[b][:, 0:128]),
                        reads=[BK[b]], writes=[("xn", dc, t0 // 128)])
            fw.fence()

            for wname, src, dst in ((("wq", wq_d, wq_sb), ("wk", wk_d, wk_sb),
                                    ("wo", wo_d, wo_sb))
                                    if "qkvo" in phases else ()):
                for dc in range(8):
                    fw.dma("sp", lambda h, src=src, dc=dc: h.dma_start(
                        out=stage_f[:, 0:1024], in_=src[dc * 128:(dc + 1) * 128, :]),
                        writes=["stage_f"], sem_key="qkvo_in")
                    fw.op("dve", lambda h, dst=dst, dc=dc: h.tensor_copy(
                        dst[:, dc * 1024:(dc + 1) * 1024], stage_f[:, 0:1024]),
                        reads=["stage_f"], writes=[(wname, dc)])

            ngrp = (NCH // GC4) if "scan" in phases else 0
            if nscan is not None:
                ngrp = min(ngrp, nscan)
            for g in range(ngrp):
                t0 = g * GC4 * CH
                for dc in range(8):
                    fw.op("act", lambda h, dc=dc, t0=t0: h.activation(
                        dump_bf[:, 0:W], xn_f[:, dc * T + t0:dc * T + t0 + W], AF.Square),
                        reads=[("xn", dc, t0 // 128 + i) for i in range(4)],
                        writes=["dump_bf"])
                    fw.op("pe", lambda h, dc=dc: h.matmul(
                        pb[5][0:1, 0:W], c["ones_cb"][:], dump_bf[:, 0:W],
                        start=(dc == 0), stop=(dc == 7)),
                        reads=["dump_bf", "ones_cb"], writes=[BK[5]], accum=True)
                fw.op("act", lambda h: h.activation(row1[0:1, 0:W], pb[5][0:1, 0:W],
                                                    AF.Ln, bias=cst[0:1, 0:1],
                                                    scale=cst[0:1, 1:2]),
                      reads=[BK[5], "cst"], writes=["row1"])
                fw.op("act", lambda h: h.activation(row2[0:1, 0:W], row1[0:1, 0:W],
                                                    AF.Exp, scale=cst[0:1, 3:4]),
                      reads=["row1", "cst"], writes=["row2"])
                fw.op("pe", lambda h: h.matmul(pb[6][:, 0:W], c["ones_r"][:],
                                               row2[0:1, 0:W], start=True, stop=True),
                      reads=["row2", "ones_r"], writes=[BK[6]], accum=True)
                for dc in range(8):
                    fw.op("dve", lambda h, dc=dc, t0=t0: h.scalar_tensor_tensor(
                        h_bf[:, dc * W:(dc + 1) * W],
                        xn_f[:, dc * T + t0:dc * T + t0 + W],
                        wn_a[:, dc:dc + 1], pb[6][:, 0:W], ALU.mult, ALU.mult),
                        reads=[("xn", dc, t0 // 128), ("wn_a", dc), BK[6]],
                        writes=[("h", dc)])
                for hh in range(H):
                    bv = hh % 2
                    fw.dma("sp", lambda h, hh=hh, bv=bv: h.dma_start(
                        out=wvsl[bv][:, :].rearrange("p (c j) -> p c j", c=8),
                        in_=wvb_d[:, hh * 128:(hh + 1) * 128].rearrange(
                            "(c p) j -> p c j", p=128)),
                        reads=[("wv_bf", hh)], writes=[("wvsl", bv)],
                        sem_key="wvs%d" % bv)
                    for wname, wsb, bi, dstk in (("wq", wq_sb, 0, "qv"),
                                                 ("wk", wk_sb, 1, "kv")):
                        for dc in range(8):
                            fw.op("pe", lambda h, wsb=wsb, bi=bi, dc=dc, hh=hh: h.matmul(
                                pb[bi][:, 0:W],
                                wsb[:, dc * 1024 + hh * 128:dc * 1024 + hh * 128 + 128],
                                h_bf[:, dc * W:(dc + 1) * W],
                                start=(dc == 0), stop=(dc == 7)),
                                reads=[(wname, dc), ("h", dc)],
                                writes=[BK[bi]], accum=True)
                    for dc in range(8):
                        fw.op("pe", lambda h, dc=dc, bv=bv: h.matmul(
                            pb[2][:, 0:W], wvsl[bv][:, dc * 128:(dc + 1) * 128],
                            h_bf[:, dc * W:(dc + 1) * W],
                            start=(dc == 0), stop=(dc == 7)),
                            reads=[("wvsl", bv), ("h", dc)],
                            writes=[BK[2]], accum=True)
                    fw.op("dve", lambda h, hh=hh: h.tensor_copy(
                        q_bf[:, hh * W:(hh + 1) * W], pb[0][:, 0:W]),
                        reads=[BK[0]], writes=[("qv", hh)])
                    fw.op("dve", lambda h, hh=hh: h.tensor_copy(
                        k_bf[:, hh * W:(hh + 1) * W], pb[1][:, 0:W]),
                        reads=[BK[1]], writes=[("kv", hh)])
                    fw.op("act", lambda h, hh=hh: h.copy(vfm_bf[:, 0:W], pb[2][:, 0:W]),
                          reads=[BK[2]], writes=["vfm"])
                    vtoks = [fw.dma("sp", lambda h, hh=hh, cc=cc: h.dma_start_transpose(
                        vtm[:, (hh * GC4 + cc) * 128:(hh * GC4 + cc) * 128 + 128],
                        vfm_bf[:, cc * 128:cc * 128 + 128]),
                        reads=["vfm"], writes=[("vtm", hh, cc)],
                        sem_key="vt%d" % hh) for cc in range(GC4)]
                    fw.alias_writer([("vtm", hh, cc) for cc in range(GC4)], vtoks[-1])
                    ktoks = [fw.dma("sp", lambda h, hh=hh, cc=cc: h.dma_start_transpose(
                        ktm[:, (hh * GC4 + cc) * 128:(hh * GC4 + cc) * 128 + 128],
                        k_bf[:, hh * W + cc * 128:hh * W + cc * 128 + 128]),
                        reads=[("kv", hh)], writes=[("ktm", hh, cc)],
                        sem_key="kt%d" % hh) for cc in range(GC4)]
                    fw.alias_writer([("ktm", hh, cc) for cc in range(GC4)], ktoks[-1])
                for cc in range(GC4):
                    gchunk = g * GC4 + cc
                    for hh in range(H):
                        baT = 3 if hh % 2 == 0 else 5
                        bo = 4 if hh % 2 == 0 else 6
                        fw.op("pe", lambda h, hh=hh, cc=cc, baT=baT: h.matmul(
                            pb[baT][:, 0:128],
                            k_bf[:, hh * W + cc * 128:hh * W + cc * 128 + 128],
                            q_bf[:, hh * W + cc * 128:hh * W + cc * 128 + 128],
                            start=True, stop=True),
                            reads=[("kv", hh), ("qv", hh)], writes=[BK[baT]],
                            accum=True)
                        fw.op("dve", lambda h, hh=hh, baT=baT: h.tensor_tensor(
                            aT_bf[:, (hh % 2) * 128:(hh % 2) * 128 + 128],
                            pb[baT][:, 0:128], c["mask"][:], op=ALU.mult),
                            reads=[BK[baT], "mask"], writes=[("aTm", hh % 2)])
                        fw.op("pe", lambda h, hh=hh, cc=cc, gchunk=gchunk, bo=bo: h.matmul(
                            pb[bo][:, 0:128],
                            vtm[:, (hh * GC4 + cc) * 128:(hh * GC4 + cc) * 128 + 128],
                            aT_bf[:, (hh % 2) * 128:(hh % 2) * 128 + 128],
                            start=True, stop=(gchunk == 0)),
                            reads=[("vtm", hh, cc), ("aTm", hh % 2)], writes=[BK[bo]],
                            accum=True)
                        if gchunk > 0:
                            fw.op("pe", lambda h, hh=hh, cc=cc, bo=bo: h.matmul(
                                pb[bo][:, 0:128], S_bf[:, hh * 128:(hh + 1) * 128],
                                q_bf[:, hh * W + cc * 128:hh * W + cc * 128 + 128],
                                start=False, stop=True),
                                reads=[("Sbf", hh), ("qv", hh)], writes=[BK[bo]],
                                accum=True)
                        fw.op("act", lambda h, hh=hh, cc=cc, bo=bo: h.copy(
                            o2_sb[:, (hh * GC4 + cc) * 128:(hh * GC4 + cc) * 128 + 128],
                            pb[bo][:, 0:128]),
                            reads=[BK[bo]], writes=[("o2", hh, cc)])
                        fw.op("pe", lambda h, hh=hh, cc=cc: h.matmul(
                            pb[2][:, 0:128],
                            ktm[:, (hh * GC4 + cc) * 128:(hh * GC4 + cc) * 128 + 128],
                            vtm[:, (hh * GC4 + cc) * 128:(hh * GC4 + cc) * 128 + 128],
                            start=True, stop=True),
                            reads=[("ktm", hh, cc), ("vtm", hh, cc)],
                            writes=[BK[2]], accum=True)
                        if gchunk == 0:
                            fw.op("dve", lambda h, hh=hh: h.tensor_copy(
                                S_sb[:, hh * 128:(hh + 1) * 128], pb[2][:, 0:128]),
                                reads=[BK[2]], writes=[("S", hh)])
                        else:
                            fw.op("dve", lambda h, hh=hh: h.tensor_tensor(
                                S_sb[:, hh * 128:(hh + 1) * 128],
                                pb[2][:, 0:128], S_sb[:, hh * 128:(hh + 1) * 128],
                                op=ALU.add),
                                reads=[BK[2], ("S", hh)], writes=[("S", hh)])
                        fw.op("act", lambda h, hh=hh: h.copy(
                            S_bf[:, hh * 128:(hh + 1) * 128],
                            S_sb[:, hh * 128:(hh + 1) * 128]),
                            reads=[("S", hh)], writes=[("Sbf", hh)])
                for hh in range(H):
                    fw.op("act", lambda h, hh=hh: h.activation(
                        dump_bf[:, 0:W], o2_sb[:, hh * W:(hh + 1) * W], AF.Square),
                        reads=[("o2", hh, i) for i in range(4)], writes=["dump_bf"])
                    fw.op("pe", lambda h: h.matmul(pb[5][0:1, 0:W], c["ones_cb"][:],
                                                   dump_bf[:, 0:W], start=True, stop=True),
                          reads=["dump_bf", "ones_cb"], writes=[BK[5]], accum=True)
                    fw.op("act", lambda h: h.activation(row1[0:1, 0:W], pb[5][0:1, 0:W],
                                                        AF.Ln, bias=cst[0:1, 0:1],
                                                        scale=cst[0:1, 2:3]),
                          reads=[BK[5], "cst"], writes=["row1"])
                    fw.op("act", lambda h: h.activation(row2[0:1, 0:W], row1[0:1, 0:W],
                                                        AF.Exp, scale=cst[0:1, 3:4]),
                          reads=["row1", "cst"], writes=["row2"])
                    fw.op("pe", lambda h: h.matmul(pb[6][:, 0:W], c["ones_r"][:],
                                                   row2[0:1, 0:W], start=True, stop=True),
                          reads=["row2", "ones_r"], writes=[BK[6]], accum=True)
                    fw.op("dve", lambda h, hh=hh: h.scalar_tensor_tensor(
                        onw_bf[:, hh * W:(hh + 1) * W], o2_sb[:, hh * W:(hh + 1) * W],
                        wn_o[:, 0:1], pb[6][:, 0:W], ALU.mult, ALU.mult),
                        reads=[("o2", hh, 0), "wn_o", BK[6]],
                        writes=[("onw", hh)])
                for dc in range(8):
                    b = dc % 2
                    for hh in range(H):
                        fw.op("pe", lambda h, dc=dc, hh=hh, b=b: h.matmul(
                            pb[b][:, 0:W],
                            wo_sb[:, hh * 1024 + dc * 128:hh * 1024 + dc * 128 + 128],
                            onw_bf[:, hh * W:(hh + 1) * W],
                            start=(hh == 0), stop=(hh == 7)),
                            reads=[("wo", hh), ("onw", hh)],
                            writes=[BK[b]], accum=True)
                    fw.op("dve", lambda h, dc=dc, t0=t0, b=b: h.tensor_tensor(
                        xn_f[:, dc * T + t0:dc * T + t0 + W], pb[b][:, 0:W],
                        xn_f[:, dc * T + t0:dc * T + t0 + W], op=ALU.add),
                        reads=[BK[b]] + [("xn", dc, t0 // 128 + i) for i in range(4)],
                        writes=[("xn", dc, t0 // 128), ("xn", dc, t0 // 128 + 1),
                                ("xn", dc, t0 // 128 + 2), ("xn", dc, t0 // 128 + 3)])
            fw.fence()

            for tt in range(T // 512 if "mlp" in phases else 0):
                t0 = tt * 512
                for dc in range(8):
                    fw.op("act", lambda h, dc=dc, t0=t0: h.activation(
                        dump_bf[:, 0:W], xn_f[:, dc * T + t0:dc * T + t0 + W], AF.Square),
                        reads=[("xn", dc, t0 // 128 + i) for i in range(4)],
                        writes=["dump_bf"])
                    fw.op("pe", lambda h, dc=dc: h.matmul(
                        pb[5][0:1, 0:W], c["ones_cb"][:], dump_bf[:, 0:W],
                        start=(dc == 0), stop=(dc == 7)),
                        reads=["dump_bf", "ones_cb"], writes=[BK[5]], accum=True)
                fw.op("act", lambda h: h.activation(row1[0:1, 0:W], pb[5][0:1, 0:W],
                                                    AF.Ln, bias=cst[0:1, 0:1],
                                                    scale=cst[0:1, 1:2]),
                      reads=[BK[5], "cst"], writes=["row1"])
                fw.op("act", lambda h: h.activation(row2[0:1, 0:W], row1[0:1, 0:W],
                                                    AF.Exp, scale=cst[0:1, 3:4]),
                      reads=["row1", "cst"], writes=["row2"])
                fw.op("pe", lambda h: h.matmul(pb[6][:, 0:W], c["ones_r"][:],
                                               row2[0:1, 0:W], start=True, stop=True),
                      reads=["row2", "ones_r"], writes=[BK[6]], accum=True)
                for dc in range(8):
                    fw.op("dve", lambda h, dc=dc, t0=t0: h.scalar_tensor_tensor(
                        h2_bf[:, dc * W:(dc + 1) * W],
                        xn_f[:, dc * T + t0:dc * T + t0 + W],
                        wn_m[:, dc:dc + 1], pb[6][:, 0:W], ALU.mult, ALU.mult),
                        reads=[("xn", dc, t0 // 128), ("wn_m", dc), BK[6]],
                        writes=[("h2", dc)])
                for icg in range(8):
                    bw = icg % 2
                    for dc in range(8):
                        fw.dma("sp", lambda h, dc=dc, icg=icg, bw=bw: h.dma_start(
                            out=wgsl[bw][:, dc * 512:(dc + 1) * 512],
                            in_=wgb_d[dc * 128:(dc + 1) * 128, icg * 512:(icg + 1) * 512]),
                            reads=[("wg_bf", dc)], writes=[("wgsl", bw, dc)],
                            sem_key="wgs%d_%d" % (bw, dc))
                        fw.dma("sp", lambda h, dc=dc, icg=icg, bw=bw: h.dma_start(
                            out=wusl[bw][:, dc * 512:(dc + 1) * 512],
                            in_=wub_d[dc * 128:(dc + 1) * 128, icg * 512:(icg + 1) * 512]),
                            reads=[("wu_bf", dc)], writes=[("wusl", bw, dc)],
                            sem_key="wus%d_%d" % (bw, dc))
                    for ici in range(4):
                        ic = icg * 4 + ici
                        for dc in range(8):
                            fw.op("pe", lambda h, dc=dc, ici=ici, bw=bw: h.matmul(
                                pb[0][:, 0:W],
                                wgsl[bw][:, dc * 512 + ici * 128:dc * 512 + ici * 128 + 128],
                                h2_bf[:, dc * W:(dc + 1) * W],
                                start=(dc == 0), stop=(dc == 7)),
                                reads=[("wgsl", bw, dc), ("h2", dc)], writes=[BK[0]],
                                accum=True)
                        for dc in range(8):
                            fw.op("pe", lambda h, dc=dc, ici=ici, bw=bw: h.matmul(
                                pb[1][:, 0:W],
                                wusl[bw][:, dc * 512 + ici * 128:dc * 512 + ici * 128 + 128],
                                h2_bf[:, dc * W:(dc + 1) * W],
                                start=(dc == 0), stop=(dc == 7)),
                                reads=[("wusl", bw, dc), ("h2", dc)], writes=[BK[1]],
                                accum=True)
                        fw.op("act", lambda h: h.activation(sil_f[:, 0:W], pb[0][:, 0:W],
                                                            AF.Silu),
                              reads=[BK[0]], writes=["sil"])
                        fw.op("dve", lambda h, ic=ic: h.tensor_tensor(
                            c_bf[:, ic * W:(ic + 1) * W], sil_f[:, 0:W], pb[1][:, 0:W],
                            op=ALU.mult),
                            reads=["sil", BK[1]], writes=[("c", ic)])
                for dc in range(8):
                    b = 2 + dc % 2
                    bw = dc % 2
                    fw.dma("sp", lambda h, dc=dc, bw=bw: h.dma_start(
                        out=wdsl[bw][:, :].rearrange("p (i j) -> p i j", i=32),
                        in_=wdb_d[:, dc * 128:(dc + 1) * 128].rearrange(
                            "(i p) j -> p i j", p=128)),
                        reads=[("wd_bf", dc)], writes=[("wdsl", bw)],
                        sem_key="wds%d" % bw)
                    for ic in range(32):
                        fw.op("pe", lambda h, dc=dc, ic=ic, b=b, bw=bw: h.matmul(
                            pb[b][:, 0:W], wdsl[bw][:, ic * 128:(ic + 1) * 128],
                            c_bf[:, ic * W:(ic + 1) * W],
                            start=(ic == 0), stop=(ic == 31)),
                            reads=[("wdsl", bw), ("c", ic)],
                            writes=[BK[b]], accum=True)
                    fw.op("dve", lambda h, dc=dc, t0=t0, b=b: h.tensor_tensor(
                        xn_f[:, dc * T + t0:dc * T + t0 + W], pb[b][:, 0:W],
                        xn_f[:, dc * T + t0:dc * T + t0 + W], op=ALU.add),
                        reads=[BK[b]] + [("xn", dc, t0 // 128 + i) for i in range(4)],
                        writes=[("xn", dc, t0 // 128), ("xn", dc, t0 // 128 + 1),
                                ("xn", dc, t0 // 128 + 2), ("xn", dc, t0 // 128 + 3)])
            fw.exempt = True
            stoks = []
            for dc in range(8):
                stoks.append(fw.dma("sp", lambda h, dc=dc, seq=seq: h.dma_start(
                    out=s_out[dc * 128:(dc + 1) * 128, seq * T:(seq + 1) * T],
                    in_=xn_f[:, dc * T:(dc + 1) * T]),
                    reads=[("xn", dc, i) for i in range(16)], sem_key="s_out"))
            fw.barrier_wait("sp", stoks)
            fw.exempt = False
            fw.fence()

        fw.exempt = True
        cross_partition_max(fw, c, mx_part[:, 0:1], AFa[:, 0:128],
                            rowb, pb[0][:, 0:128], mrow[0:1, 0:1],
                            [("cpm", 0), BK[0], ("cpm", 2), ("cpm", 3), "mx_part"])
        tk = fw.dma("sp", lambda h: h.dma_start(out=mx_out, in_=mrow[0:1, 0:1]),
                    reads=[("cpm", 3)], sem_key="mx_o")
        fw.barrier_wait("sp", [tk])
        fw.finalize(block)
    return nc



def emit_scales(fw, nc, parts, scal, scal_pp, bounce_d, specs):
    raise NotImplementedError


def qdq_grid(fw, tiles, src_ap, rk, dst_i, W, tag, extra_reads=()):
    r_ap, r_key = rk
    tmp = tiles["qtmp"]
    fw.op("dve", lambda h: h.tensor_scalar(tmp[:, 0:W], src_ap, r_ap, MAGIC,
                                           ALU.mult, ALU.add),
          reads=[tag + "_src", r_key] + list(extra_reads), writes=[tag + "_t"])
    fw.op("dve", lambda h: h.tensor_scalar(dst_i[:, 0:W], tmp[:, 0:W], MAGIC,
                                           None, ALU.subtract),
          reads=[tag + "_t"], writes=[tag + "_i"])



def build_f1():
    nc = bass.Bass("TRN2", target_bir_lowering=False)
    x_d = nc.dram_tensor("x", [NTOK, D], F32, kind="ExternalInput").ap()
    mx8_d = nc.dram_tensor("mx8", [1, 8], F32, kind="ExternalInput").ap()
    wu_d = nc.dram_tensor("w_up", [INTER_F, D], F32, kind="ExternalInput").ap()
    wg_d = nc.dram_tensor("w_gate", [INTER_F, D], F32, kind="ExternalInput").ap()
    pu_out = nc.dram_tensor("P_up", [INTER_F, NTOK], F32, kind="ExternalOutput").ap()
    pg_out = nc.dram_tensor("P_gate", [INTER_F, NTOK], F32, kind="ExternalOutput").ap()
    mpu_out = nc.dram_tensor("m_pu", [1, 1], F32, kind="ExternalOutput").ap()
    mpg_out = nc.dram_tensor("m_pg", [1, 1], F32, kind="ExternalOutput").ap()
    sc_d = nc.dram_tensor("scb", [1, 8], F32).ap()

    es = ExitStack()
    sb = lambda n, s, dt=F32: es.enter_context(nc.sbuf_tensor(n, s, dt))
    ps = lambda n, s, dt=F32: es.enter_context(nc.psum_tensor(n, s, dt))
    with es:
        wtu = sb("wtu", [128, 8 * INTER_F], BF16)
        wtg = sb("wtg", [128, 8 * INTER_F], BF16)
        xg_bf = sb("xg_bf", [128, 8 * 512], BF16)
        stage = sb("stage", [128, 1024])
        blk = sb("blk", [128, 32])
        rblk = sb("rblk", [128, 32])
        qt = sb("qt", [128, 1024])
        gi = sb("gi", [128, 1024], I32)
        wdq = sb("wdq", [128, 1024], BF16)
        x_tm = sb("x_tm", [128, 1024])
        dump = sb("dump", [128, 1024])
        stats = sb("stats", [128, 8])
        ev = [sb("ev%d" % i, [128, 512]) for i in range(2)]
        mx1 = sb("mx1", [128, 1])
        mx2 = sb("mx2", [128, 1])
        cpp = sb("cpp", [128, 1])
        tmp128 = sb("tmp128", [128, 128])
        rowb = sb("rowb", [1, 128])
        mrow = sb("mrow", [1, 2])
        scal = sb("scal", [1, 8])
        mxr = sb("mxr", [1, 8])
        scal_pp = sb("scal_pp", [128, 8])
        cons = {"ident_f": sb("ident_f", [128, 128]),
                "ident_b": sb("ident_b", [128, 128], BF16),
                "mask": sb("mask", [128, 128]),
                "ones_c": sb("ones_c", [128, 1]),
                "ones_r": sb("ones_r", [1, 128])}
        c7 = sb("c7", [1, 1])
        pbk = [ps("pbk%d" % i, [128, 512]) for i in range(6)]
        pTb = ps("pTb", [128, 1024], BF16)
        fw = FW(nc, es)
        block = es.enter_context(nc.Block())

        emit_consts(fw, cons)
        fw.op("dve", lambda h: h.memset(scal[:], 0.0),
              writes=[("sc", i) for i in range(8)])
        fw.op("dve", lambda h: h.memset(mx1[:], 0.0), writes=["mx1"])
        fw.op("dve", lambda h: h.memset(mx2[:], 0.0), writes=["mx2"])
        fw.op("dve", lambda h: h.memset(cpp[:, 0:1], EPS), writes=["cpp"])

        fw.dma("sp", lambda h: h.dma_start(out=mxr[0:1, 0:8], in_=mx8_d),
               writes=["mx8"], sem_key="mx8")
        fw.op("dve", lambda h: h.reduce_max(scal[0:1, 2:3], mxr[0:1, 0:8], axis=AX.X),
              reads=["mx8"], writes=[("sc", 2)])
        fw.op("dve", lambda h: h.memset(c7[:], QMAX), writes=["c7"])
        fw.op("dve", lambda h: h.reciprocal(scal[0:1, 3:4], c7[0:1, 0:1]),
              reads=["c7"], writes=[("sc", 3)])
        fw.op("dve", lambda h: h.tensor_tensor(scal[0:1, 4:5], scal[0:1, 2:3],
                                               scal[0:1, 3:4], op=ALU.mult),
              reads=[("sc", 2), ("sc", 3)], writes=[("sc", 4)])
        fw.op("dve", lambda h: h.tensor_scalar(scal[0:1, 4:5], scal[0:1, 4:5],
                                               1e-12, None, ALU.max),
              reads=[("sc", 4)], writes=[("sc", 4)])
        fw.op("dve", lambda h: h.reciprocal(scal[0:1, 5:6], scal[0:1, 4:5]),
              reads=[("sc", 4)], writes=[("sc", 5)])
        t1 = fw.dma("sp", lambda h: h.dma_start(out=sc_d, in_=scal[0:1, 0:8]),
                    reads=[("sc", i) for i in range(8)], writes=["scb"], sem_key="scb")
        fw.dma("sp", lambda h: h.dma_start(out=scal_pp[:, 0:8],
                                           in_=sc_d.partition_broadcast(128)),
               reads=["scb"], writes=["scal_pp"], sem_key="scpp")

        fw.op("dve", lambda h: h.memset(c7[:], 7.0), reads=[("sc", 3)], writes=["c7"])
        fw.op("dve", lambda h: h.reciprocal(scal[0:1, 6:7], c7[0:1, 0:1]),
              reads=["c7"], writes=[("sc", 6)])
        t2 = fw.dma("sp", lambda h: h.dma_start(out=sc_d, in_=scal[0:1, 0:8]),
                    reads=[("sc", i) for i in range(8)], writes=["scb"],
                    sem_key="scb")
        fw.dma("sp", lambda h: h.dma_start(out=scal_pp[:, 0:8],
                                           in_=sc_d.partition_broadcast(128)),
               reads=["scb"], writes=["scal_pp"], sem_key="scpp")
        for wname, src, dst in (("u", wu_d, wtu), ("g", wg_d, wtg)):
            for rt in range(INTER_F // 128):
                fw.dma("sp", lambda h, src=src, rt=rt: h.dma_start(
                    out=stage[:, :], in_=src[rt * 128:(rt + 1) * 128, :]),
                    writes=["stage"], sem_key="w_in")
                fw.op("dve", lambda h: h.reduce_max(
                    blk[:, :, None], stage[:].rearrange("p (b k) -> p b k", b=32),
                    axis=AX.X, apply_absolute_value=True),
                    reads=["stage"], writes=["blk"])
                fw.op("dve", lambda h: h.tensor_scalar(blk[:], blk[:],
                                                       scal_pp[:, 6:7], 1e-12,
                                                       ALU.mult, ALU.max),
                      reads=["blk", "scal_pp"], writes=["blk"])
                fw.op("dve", lambda h: h.reciprocal(rblk[:], blk[:]),
                      reads=["blk"], writes=["rblk"])
                fw.op("dve", lambda h: h.tensor_tensor(
                    qt[:].rearrange("p (b k) -> p b k", b=32),
                    stage[:].rearrange("p (b k) -> p b k", b=32),
                    rblk[:, :, None].broadcast_to([128, 32, 32]), op=ALU.mult),
                    reads=["stage", "rblk"], writes=["qt"])
                fw.op("dve", lambda h: h.tensor_scalar(qt[:], qt[:], MAGIC, MAGIC,
                                                       ALU.add, ALU.subtract),
                      reads=["qt"], writes=["qt"])
                fw.op("dve", lambda h: h.tensor_copy(gi[:], qt[:]),
                      reads=["qt"], writes=["gi"])
                fw.op("dve", lambda h: h.tensor_tensor(
                    wdq[:].rearrange("p (b k) -> p b k", b=32),
                    gi[:].rearrange("p (b k) -> p b k", b=32),
                    blk[:, :, None].broadcast_to([128, 32, 32]), op=ALU.mult),
                    reads=["gi", "blk"], writes=["wdq"])
                for dc in range(8):
                    fw.op("pe", lambda h, dc=dc: h.transpose(
                        pTb[:, (dc % 8) * 128:(dc % 8) * 128 + 128],
                        wdq[:, dc * 128:(dc + 1) * 128], cons["ident_b"][:]),
                        reads=["wdq", "ident_b"], writes=["pTbank"])
                    fw.op("dve", lambda h, dst=dst, dc=dc, rt=rt: h.tensor_copy(
                        dst[:, dc * INTER_F + rt * 128:dc * INTER_F + rt * 128 + 128],
                        pTb[:, (dc % 8) * 128:(dc % 8) * 128 + 128]),
                        reads=["pTbank"], writes=[("wt" + wname, dc, rt)])
        fw.fence()

        for tt in range(NTOK // 512):
            t0 = tt * 512
            for gg in range(4):
                tg = t0 + gg * 128
                fw.dma("sp", lambda h, tg=tg: h.dma_start(
                    out=x_tm[:, :], in_=x_d[tg:tg + 128, :]),
                    writes=["x_tm"], sem_key="x_in")
                fw.op("dve", lambda h: h.reduce_sum(stats[:, 0:1], x_tm[:, :], axis=AX.X),
                      reads=["x_tm"], writes=[("st", 0)])
                fw.op("dve", lambda h: h.tensor_scalar(stats[:, 1:2], stats[:, 0:1],
                                                       1.0 / D, None, ALU.mult),
                      reads=[("st", 0)], writes=[("st", 1)])
                fw.op("act", lambda h: h.activation(dump[:, :], x_tm[:, :], AF.Square,
                                                    accum_out=stats[:, 2:3]),
                      reads=["x_tm"], writes=["dump", ("st", 2)])
                fw.op("dve", lambda h: h.tensor_tensor(stats[:, 3:4], stats[:, 1:2],
                                                       stats[:, 1:2], op=ALU.mult),
                      reads=[("st", 1)], writes=[("st", 3)])
                fw.op("dve", lambda h: h.tensor_scalar(stats[:, 4:5], stats[:, 2:3],
                                                       1.0 / D, None, ALU.mult),
                      reads=[("st", 2)], writes=[("st", 4)])
                fw.op("dve", lambda h: h.tensor_tensor(stats[:, 5:6], stats[:, 4:5],
                                                       stats[:, 3:4], op=ALU.subtract),
                      reads=[("st", 3), ("st", 4)], writes=[("st", 5)])
                fw.op("act", lambda h: h.activation(stats[:, 6:7], stats[:, 5:6],
                                                    AF.Sqrt, bias=cpp[:, 0:1]),
                      reads=[("st", 5), "cpp"], writes=[("st", 6)])
                fw.op("dve", lambda h: h.reciprocal(stats[:, 7:8], stats[:, 6:7]),
                      reads=[("st", 6)], writes=[("st", 7)])
                fw.op("dve", lambda h: h.tensor_scalar(x_tm[:, :], x_tm[:, :],
                                                       stats[:, 1:2], stats[:, 7:8],
                                                       ALU.subtract, ALU.mult),
                      reads=["x_tm", ("st", 1), ("st", 7)], writes=["x_tm"])
                for dc in range(8):
                    bk = pbk[4 + dc % 2]
                    fw.op("pe", lambda h, dc=dc, bk=bk: h.transpose(
                        bk[:, 0:128], x_tm[:, dc * 128:(dc + 1) * 128],
                        cons["ident_f"][:]),
                        reads=["x_tm", "ident_f"], writes=[("ptT", dc % 2)])
                    fw.op("dve", lambda h, dc=dc, bk=bk: h.tensor_scalar(
                        qt[:, 0:128], bk[:, 0:128], scal_pp[:, 5:6], MAGIC,
                        ALU.mult, ALU.add),
                        reads=[("ptT", dc % 2), "scal_pp"], writes=["qt"])
                    fw.op("dve", lambda h: h.tensor_scalar(
                        gi[:, 0:128], qt[:, 0:128], MAGIC, None, ALU.subtract),
                        reads=["qt"], writes=["gi"])
                    fw.op("dve", lambda h, dc=dc, gg=gg: h.tensor_copy(
                        xg_bf[:, dc * 512 + gg * 128:dc * 512 + gg * 128 + 128],
                        gi[:, 0:128]),
                        reads=["gi"], writes=[("xg", dc, gg)])
            for wname, wt, out_d, mxt, mxk in (("u", wtu, pu_out, mx1, "mx1"),
                                               ("g", wtg, pg_out, mx2, "mx2")):
                for oc in range(INTER_F // 128):
                    bk = pbk[(0 if wname == "u" else 2) + oc % 2]
                    for dc in range(8):
                        fw.op("pe", lambda h, wt=wt, oc=oc, dc=dc, bk=bk: h.matmul(
                            bk[:, 0:512],
                            wt[:, dc * INTER_F + oc * 128:dc * INTER_F + oc * 128 + 128],
                            xg_bf[:, dc * 512:(dc + 1) * 512],
                            start=(dc == 0), stop=(dc == 7)),
                            reads=[("wt" + wname, dc, oc)] + [("xg", dc, g) for g in range(4)],
                            writes=[("pmm", wname, oc % 2)], accum=True)
                    eb = oc % 2
                    fw.op("act", lambda h, bk=bk, eb=eb: h.copy(ev[eb][:, 0:512],
                                                                bk[:, 0:512]),
                          reads=[("pmm", wname, oc % 2)], writes=[("ev", eb)])
                    fw.op("dve", lambda h, eb=eb, wname=wname: h.reduce_max(
                        stats[:, 2 + (0 if wname == "u" else 1):3 + (0 if wname == "u" else 1)],
                        ev[eb][:, 0:512], axis=AX.X, apply_absolute_value=True),
                        reads=[("ev", eb)], writes=[("st", 2, wname)])
                    fw.op("dve", lambda h, mxt=mxt, wname=wname: h.tensor_tensor(
                        mxt[:], mxt[:],
                        stats[:, 2 + (0 if wname == "u" else 1):3 + (0 if wname == "u" else 1)],
                        op=ALU.max),
                        reads=[("st", 2, wname), mxk], writes=[mxk])
                    fw.dma("sp", lambda h, out_d=out_d, oc=oc, t0=t0, eb=eb: h.dma_start(
                        out=out_d[oc * 128:(oc + 1) * 128, t0:t0 + 512],
                        in_=ev[eb][:, 0:512]),
                        reads=[("ev", eb)], sem_key="p_out%d" % eb)
        fw.fence()
        cross_partition_max(fw, cons, mx1[:, 0:1], tmp128, rowb, pbk[0][:, 0:128],
                            mrow[0:1, 0:1], [("cp", 0), ("cp", 1), ("cp", 2), ("cp", 3), "mx1"])
        cross_partition_max(fw, cons, mx2[:, 0:1], tmp128, rowb, pbk[1][:, 0:128],
                            mrow[0:1, 1:2], [("cq", 0), ("cq", 1), ("cq", 2), ("cq", 3), "mx2"])
        tks = [fw.dma("sp", lambda h: h.dma_start(out=mpu_out, in_=mrow[0:1, 0:1]),
                      reads=[("cp", 3)], sem_key="m_o"),
               fw.dma("sp", lambda h: h.dma_start(out=mpg_out, in_=mrow[0:1, 1:2]),
                      reads=[("cq", 3)], sem_key="m_o")]
        fw.barrier_wait("sp", tks)
        fw.finalize(block)
    return nc



def build_f2():
    nc = bass.Bass("TRN2", target_bir_lowering=False)
    pg_d = nc.dram_tensor("P_gate", [INTER_F, NTOK], F32, kind="ExternalInput").ap()
    mx8_d = nc.dram_tensor("mx8", [1, 8], F32, kind="ExternalInput").ap()
    mpg8_d = nc.dram_tensor("mpg8", [1, 8], F32, kind="ExternalInput").ap()
    g_out = nc.dram_tensor("G", [INTER_F, NTOK], I16, kind="ExternalOutput").ap()
    s_out = nc.dram_tensor("S", [INTER_F, NTOK], U16, kind="ExternalOutput").ap()
    mp2_out = nc.dram_tensor("m_p2", [1, 1], F32, kind="ExternalOutput").ap()
    sc_d = nc.dram_tensor("scb", [1, 8], F32).ap()

    es = ExitStack()
    sb = lambda n, s, dt=F32: es.enter_context(nc.sbuf_tensor(n, s, dt))
    ps = lambda n, s, dt=F32: es.enter_context(nc.psum_tensor(n, s, dt))
    with es:
        W = 2048
        pg_t = [sb("pg_t%d" % i, [128, W]) for i in range(2)]
        qt = [sb("qt_%d" % i, [128, W]) for i in range(2)]
        qt2 = [sb("qt2_%d" % i, [128, W]) for i in range(2)]
        g_i = [sb("g_i%d" % i, [128, W], I16) for i in range(2)]
        sig = [sb("sig_%d" % i, [128, W]) for i in range(2)]
        s_u = [sb("s_u%d" % i, [128, W], U16) for i in range(2)]
        p2 = [sb("p2_%d" % i, [128, W]) for i in range(2)]
        stats = sb("stats", [128, 4])
        mx1 = sb("mx1", [128, 1])
        tmp128 = sb("tmp128", [128, 128])
        rowb = sb("rowb", [1, 128])
        mrow = sb("mrow", [1, 1])
        scal = sb("scal", [1, 12])
        scal_pp = sb("scal_pp", [128, 12])
        cons = {"ident_f": sb("ident_f", [128, 128]),
                "ident_b": sb("ident_b", [128, 128], BF16),
                "mask": sb("mask", [128, 128]),
                "ones_c": sb("ones_c", [128, 1]),
                "ones_r": sb("ones_r", [1, 128])}
        c7 = sb("c7", [1, 1])
        mgc = sb("mgc", [128, 2])
        pbk = ps("pbk", [128, 512])
        fw = FW(nc, es)
        block = es.enter_context(nc.Block())
        emit_consts(fw, cons)
        fw.op("dve", lambda h: h.memset(scal[:], 0.0),
              writes=[("sc", i) for i in range(12)])
        fw.op("dve", lambda h: h.memset(mgc[:, 0:1], MAGIC), writes=["mgc"])
        fw.op("dve", lambda h: h.memset(mgc[:, 1:2], 65536.0), reads=["mgc"], writes=["mgc"])
        fw.op("dve", lambda h: h.memset(mx1[:], 0.0), writes=["mx1"])

        mxr = sb("mxr", [1, 8])
        mpg = sb("mpg", [1, 8])
        fw.dma("sp", lambda h: h.dma_start(out=mxr[0:1, 0:8], in_=mx8_d),
               writes=["mx8"], sem_key="mx8")
        fw.dma("sp", lambda h: h.dma_start(out=mpg[0:1, 0:8], in_=mpg8_d),
               writes=["mpg8"], sem_key="mpg8")
        fw.op("dve", lambda h: h.memset(c7[:], QMAX), writes=["c7"])
        fw.op("dve", lambda h: h.reciprocal(scal[0:1, 0:1], c7[0:1, 0:1]),
              reads=["c7"], writes=[("sc", 0)])
        fw.op("dve", lambda h: h.reduce_max(scal[0:1, 1:2], mxr[0:1, 0:8], axis=AX.X),
              reads=["mx8"], writes=[("sc", 1)])
        fw.op("dve", lambda h: h.reduce_max(scal[0:1, 2:3], mpg[0:1, 0:8], axis=AX.X),
              reads=["mpg8"], writes=[("sc", 2)])
        fw.op("dve", lambda h: h.tensor_tensor(scal[0:1, 3:4], scal[0:1, 1:2],
                                               scal[0:1, 0:1], op=ALU.mult),
              reads=[("sc", 1), ("sc", 0)], writes=[("sc", 3)])
        fw.op("dve", lambda h: h.tensor_scalar(scal[0:1, 3:4], scal[0:1, 3:4],
                                               1e-12, None, ALU.max),
              reads=[("sc", 3)], writes=[("sc", 3)])
        fw.op("dve", lambda h: h.tensor_tensor(scal[0:1, 4:5], scal[0:1, 3:4],
                                               scal[0:1, 2:3], op=ALU.mult),
              reads=[("sc", 3), ("sc", 2)], writes=[("sc", 4)])
        fw.op("dve", lambda h: h.tensor_tensor(scal[0:1, 5:6], scal[0:1, 4:5],
                                               scal[0:1, 0:1], op=ALU.mult),
              reads=[("sc", 4), ("sc", 0)], writes=[("sc", 5)])
        fw.op("dve", lambda h: h.tensor_scalar(scal[0:1, 5:6], scal[0:1, 5:6],
                                               1e-12, None, ALU.max),
              reads=[("sc", 5)], writes=[("sc", 5)])
        fw.op("dve", lambda h: h.reciprocal(scal[0:1, 6:7], scal[0:1, 5:6]),
              reads=[("sc", 5)], writes=[("sc", 6)])
        fw.op("dve", lambda h: h.tensor_tensor(scal[0:1, 7:8], scal[0:1, 3:4],
                                               scal[0:1, 6:7], op=ALU.mult),
              reads=[("sc", 3), ("sc", 6)], writes=[("sc", 7)])
        tb = fw.dma("sp", lambda h: h.dma_start(out=sc_d, in_=scal[0:1, 0:8]),
                    reads=[("sc", i) for i in range(8)], writes=["scb"], sem_key="scb")
        fw.dma("sp", lambda h: h.dma_start(out=scal_pp[:, 0:8],
                                           in_=sc_d.partition_broadcast(128)),
               reads=["scb"], writes=["scal_pp"], sem_key="scpp")

        for ti, (oc, tt) in enumerate((oc, tt) for oc in range(INTER_F // 128)
                                      for tt in range(NTOK // W)):
            b = ti % 2
            t0 = tt * W
            fw.dma("sp", lambda h, oc=oc, t0=t0, b=b: h.dma_start(
                out=pg_t[b][:, :], in_=pg_d[oc * 128:(oc + 1) * 128, t0:t0 + W]),
                writes=[("pg", b)], sem_key="pg%d" % b)
            fw.op("act", lambda h, b=b: h.activation(qt[b][:, :], pg_t[b][:, :],
                                                     AF.Identity,
                                                     scale=scal_pp[:, 7:8],
                                                     bias=mgc[:, 0:1]),
                  reads=[("pg", b), "scal_pp", "mgc"], writes=[("qt", b)])
            fw.op("dve", lambda h, b=b: h.tensor_scalar(g_i[b][:, :], qt[b][:, :], MAGIC,
                                                        None, ALU.subtract),
                  reads=[("qt", b)], writes=[("gi", b)])
            fw.op("act", lambda h, b=b: h.activation(sig[b][:, :], g_i[b][:, :],
                                                     AF.Sigmoid,
                                                     scale=scal_pp[:, 5:6]),
                  reads=[("gi", b), "scal_pp"], writes=[("sig", b)])
            fw.op("act", lambda h, b=b: h.activation(qt2[b][:, :], sig[b][:, :], AF.Identity,
                                                scale=mgc[:, 1:2], bias=mgc[:, 0:1]),
                  reads=[("sig", b), "mgc"], writes=[("qt2", b)])
            fw.op("dve", lambda h, b=b: h.tensor_scalar(s_u[b][:, :], qt2[b][:, :], MAGIC,
                                                        65535.0, ALU.subtract, ALU.min),
                  reads=[("qt2", b)], writes=[("su", b)])
            fw.op("dve", lambda h, b=b: h.tensor_tensor(p2[b][:, :], g_i[b][:, :],
                                                        s_u[b][:, :], op=ALU.mult),
                  reads=[("gi", b), ("su", b)], writes=[("p2", b)])
            fw.op("dve", lambda h, b=b: h.reduce_max(stats[:, b:b+1], p2[b][:, :], axis=AX.X,
                                                apply_absolute_value=True),
                  reads=[("p2", b)], writes=[("st", b)])
            fw.op("dve", lambda h, b=b: h.tensor_tensor(mx1[:], mx1[:], stats[:, b:b+1],
                                                   op=ALU.max),
                  reads=[("st", b), "mx1"], writes=["mx1"])
            fw.dma("pool", lambda h, oc=oc, t0=t0, b=b: h.dma_start(
                out=g_out[oc * 128:(oc + 1) * 128, t0:t0 + W], in_=g_i[b][:, :]),
                reads=[("gi", b)], sem_key="go%d" % b)
            fw.dma("pool", lambda h, oc=oc, t0=t0, b=b: h.dma_start(
                out=s_out[oc * 128:(oc + 1) * 128, t0:t0 + W], in_=s_u[b][:, :]),
                reads=[("su", b)], sem_key="so%d" % b)
        fw.fence()
        cross_partition_max(fw, cons, mx1[:, 0:1], tmp128, rowb, pbk[:, 0:128],
                            mrow[0:1, 0:1], [("cp", 0), ("cp", 1), ("cp", 2), ("cp", 3), "mx1"])
        tk = fw.dma("sp", lambda h: h.dma_start(out=mp2_out, in_=mrow[0:1, 0:1]),
                    reads=[("cp", 3)], sem_key="m_o")
        fw.barrier_wait("sp", [tk])
        fw.finalize(block)
    return nc



def build_f3():
    nc = bass.Bass("TRN2", target_bir_lowering=False)
    pu_d = nc.dram_tensor("P_up", [INTER_F, NTOK], F32, kind="ExternalInput").ap()
    g_d = nc.dram_tensor("G", [INTER_F, NTOK], I16, kind="ExternalInput").ap()
    s_d = nc.dram_tensor("S", [INTER_F, NTOK], U16, kind="ExternalInput").ap()
    mx8_d = nc.dram_tensor("mx8", [1, 8], F32, kind="ExternalInput").ap()
    mpu8_d = nc.dram_tensor("mpu8", [1, 8], F32, kind="ExternalInput").ap()
    mpg8_d = nc.dram_tensor("mpg8", [1, 8], F32, kind="ExternalInput").ap()
    mp28_d = nc.dram_tensor("mp28", [1, 8], F32, kind="ExternalInput").ap()
    u_out = nc.dram_tensor("U", [INTER_F, NTOK], I16, kind="ExternalOutput").ap()
    g2_out = nc.dram_tensor("G2", [INTER_F, NTOK], I16, kind="ExternalOutput").ap()
    mo_out = nc.dram_tensor("m_o", [1, 1], F32, kind="ExternalOutput").ap()
    sc_d = nc.dram_tensor("scb", [1, 16], F32).ap()

    es = ExitStack()
    sb = lambda n, s, dt=F32: es.enter_context(nc.sbuf_tensor(n, s, dt))
    ps = lambda n, s, dt=F32: es.enter_context(nc.psum_tensor(n, s, dt))
    with es:
        W = 2048
        pu_t = [sb("pu_t%d" % i, [128, W]) for i in range(2)]
        g_t = [sb("g_t%d" % i, [128, W], I16) for i in range(2)]
        s_t = [sb("s_t%d" % i, [128, W], U16) for i in range(2)]
        u_i = [sb("u_i%d" % i, [128, W], I16) for i in range(2)]
        g2_i = [sb("g2_i%d" % i, [128, W], I16) for i in range(2)]
        qt = [sb("qt_%d" % i, [128, W]) for i in range(2)]
        qt2 = [sb("qt2_%d" % i, [128, W]) for i in range(2)]
        gq = [sb("gq_%d" % i, [128, W]) for i in range(2)]
        gq2 = [sb("gq2_%d" % i, [128, W]) for i in range(2)]
        sg = [sb("sg_%d" % i, [128, W]) for i in range(2)]
        sg2 = [sb("sg2_%d" % i, [128, W]) for i in range(2)]
        g2r = [sb("g2r_%d" % i, [128, W]) for i in range(2)]
        orw = [sb("orw_%d" % i, [128, W]) for i in range(2)]
        stats = sb("stats", [128, 4])
        mx1 = sb("mx1", [128, 1])
        tmp128 = sb("tmp128", [128, 128])
        rowb = sb("rowb", [1, 128])
        mrow = sb("mrow", [1, 1])
        scal = sb("scal", [1, 16])
        scal_pp = sb("scal_pp", [128, 16])
        m8 = sb("m8", [1, 32])
        cons = {"ident_f": sb("ident_f", [128, 128]),
                "ident_b": sb("ident_b", [128, 128], BF16),
                "mask": sb("mask", [128, 128]),
                "ones_c": sb("ones_c", [128, 1]),
                "ones_r": sb("ones_r", [1, 128])}
        c7 = sb("c7", [1, 1])
        mgc = sb("mgc", [128, 1])
        pbk = ps("pbk", [128, 512])
        fw = FW(nc, es)
        block = es.enter_context(nc.Block())
        emit_consts(fw, cons)
        fw.op("dve", lambda h: h.memset(scal[:], 0.0),
              writes=[("sc", i) for i in range(16)])
        fw.op("dve", lambda h: h.memset(mgc[:, 0:1], MAGIC), writes=["mgc"])
        fw.op("dve", lambda h: h.memset(mx1[:], 0.0), writes=["mx1"])

        for i, d in enumerate((mx8_d, mpu8_d, mpg8_d, mp28_d)):
            fw.dma("sp", lambda h, i=i, d=d: h.dma_start(out=m8[0:1, i * 8:(i + 1) * 8],
                                                         in_=d),
                   writes=[("m8", i)], sem_key="m8_%d" % i)
        fw.op("dve", lambda h: h.memset(c7[:], QMAX), writes=["c7"])
        fw.op("dve", lambda h: h.reciprocal(scal[0:1, 0:1], c7[0:1, 0:1]),
              reads=["c7"], writes=[("sc", 0)])
        for i, col in ((0, 1), (1, 2), (2, 3), (3, 4)):
            fw.op("dve", lambda h, i=i, col=col: h.reduce_max(
                scal[0:1, col:col + 1], m8[0:1, i * 8:(i + 1) * 8], axis=AX.X),
                reads=[("m8", i)], writes=[("sc", col)])

        def scmul(dst, a, b_):
            fw.op("dve", lambda h: h.tensor_tensor(scal[0:1, dst:dst + 1],
                                                   scal[0:1, a:a + 1],
                                                   scal[0:1, b_:b_ + 1], op=ALU.mult),
                  reads=[("sc", a), ("sc", b_)], writes=[("sc", dst)])

        def scmax12(col):
            fw.op("dve", lambda h: h.tensor_scalar(scal[0:1, col:col + 1],
                                                   scal[0:1, col:col + 1], 1e-12,
                                                   None, ALU.max),
                  reads=[("sc", col)], writes=[("sc", col)])

        def screcip(dst, a):
            fw.op("dve", lambda h: h.reciprocal(scal[0:1, dst:dst + 1],
                                                scal[0:1, a:a + 1]),
                  reads=[("sc", a)], writes=[("sc", dst)])

        scmul(5, 1, 0); scmax12(5)
        scmul(6, 5, 2)
        scmul(6, 6, 0); scmax12(6)
        screcip(8, 6)
        scmul(7, 5, 8)
        scmul(9, 5, 3); scmul(9, 9, 0); scmax12(9)
        fw.op("dve", lambda h: h.tensor_scalar(scal[0:1, 10:11], scal[0:1, 9:10],
                                               1.0 / 65536.0, None, ALU.mult),
              reads=[("sc", 9)], writes=[("sc", 10)])
        scmul(10, 10, 4); scmul(10, 10, 0); scmax12(10)
        screcip(11, 10)
        tb = fw.dma("sp", lambda h: h.dma_start(out=sc_d, in_=scal[0:1, 0:16]),
                    reads=[("sc", i) for i in range(16)],
                    writes=["scb"], sem_key="scb")
        fw.dma("sp", lambda h: h.dma_start(out=scal_pp[:, 0:16],
                                           in_=sc_d.partition_broadcast(128)),
               reads=["scb"], writes=["scal_pp"], sem_key="scpp")

        for ti, (oc, tt) in enumerate((oc, tt) for oc in range(INTER_F // 128)
                                      for tt in range(NTOK // W)):
            b = ti % 2
            t0 = tt * W
            fw.dma("sp", lambda h, oc=oc, t0=t0, b=b: h.dma_start(
                out=pu_t[b][:, :], in_=pu_d[oc * 128:(oc + 1) * 128, t0:t0 + W]),
                writes=[("pu", b)], sem_key="pu%d" % b)
            fw.dma("sp", lambda h, oc=oc, t0=t0, b=b: h.dma_start(
                out=g_t[b][:, :], in_=g_d[oc * 128:(oc + 1) * 128, t0:t0 + W]),
                writes=[("g", b)], sem_key="g%d" % b)
            fw.dma("sp", lambda h, oc=oc, t0=t0, b=b: h.dma_start(
                out=s_t[b][:, :], in_=s_d[oc * 128:(oc + 1) * 128, t0:t0 + W]),
                writes=[("s", b)], sem_key="s%d" % b)
            fw.op("act", lambda h, b=b: h.activation(qt[b][:, :], pu_t[b][:, :],
                                                     AF.Identity,
                                                     scale=scal_pp[:, 7:8],
                                                     bias=mgc[:, 0:1]),
                  reads=[("pu", b), "scal_pp", "mgc"], writes=[("qt", b)])
            fw.op("dve", lambda h, b=b: h.tensor_scalar(u_i[b][:, :], qt[b][:, :], MAGIC,
                                                        None, ALU.subtract),
                  reads=[("qt", b)], writes=[("ui", b)])
            fw.op("act", lambda h, b=b: h.activation(gq[b][:, :], g_t[b][:, :],
                                                     AF.Identity,
                                                     scale=scal_pp[:, 9:10]),
                  reads=[("g", b), "scal_pp"], writes=[("gq", b)])
            fw.op("dve", lambda h, b=b: h.tensor_scalar(sg[b][:, :], s_t[b][:, :],
                                                        1.0 / 65536.0, None, ALU.mult),
                  reads=[("s", b)], writes=[("sgv", b)])
            fw.op("dve", lambda h, b=b: h.tensor_tensor(g2r[b][:, :], gq[b][:, :], sg[b][:, :],
                                                   op=ALU.mult),
                  reads=[("gq", b), ("sgv", b)], writes=[("g2r", b)])
            fw.op("act", lambda h, b=b: h.activation(qt2[b][:, :], g2r[b][:, :], AF.Identity,
                                                scale=scal_pp[:, 11:12],
                                                bias=mgc[:, 0:1]),
                  reads=[("g2r", b), "scal_pp", "mgc"], writes=[("qt2", b)])
            fw.op("dve", lambda h, b=b: h.tensor_scalar(g2_i[b][:, :], qt2[b][:, :], MAGIC,
                                                        None, ALU.subtract),
                  reads=[("qt2", b)], writes=[("g2i", b)])
            fw.op("act", lambda h, b=b: h.activation(gq2[b][:, :], g2_i[b][:, :],
                                                     AF.Identity,
                                                     scale=scal_pp[:, 10:11]),
                  reads=[("g2i", b), "scal_pp"], writes=[("gq2", b)])
            fw.op("dve", lambda h, b=b: h.tensor_scalar(sg2[b][:, :], u_i[b][:, :],
                                                        scal_pp[:, 6:7], None,
                                                        ALU.mult),
                  reads=[("ui", b), "scal_pp"], writes=[("sgv2", b)])
            fw.op("dve", lambda h, b=b: h.tensor_tensor(orw[b][:, :], gq2[b][:, :], sg2[b][:, :],
                                                   op=ALU.mult),
                  reads=[("gq2", b), ("sgv2", b)], writes=[("orw", b)])
            fw.op("dve", lambda h, b=b: h.reduce_max(stats[:, b:b+1], orw[b][:, :], axis=AX.X,
                                                apply_absolute_value=True),
                  reads=[("orw", b)], writes=[("st", b)])
            fw.op("dve", lambda h, b=b: h.tensor_tensor(mx1[:], mx1[:], stats[:, b:b+1],
                                                   op=ALU.max),
                  reads=[("st", b), "mx1"], writes=["mx1"])
            fw.dma("pool", lambda h, oc=oc, t0=t0, b=b: h.dma_start(
                out=u_out[oc * 128:(oc + 1) * 128, t0:t0 + W], in_=u_i[b][:, :]),
                reads=[("ui", b)], sem_key="uo%d" % b)
            fw.dma("pool", lambda h, oc=oc, t0=t0, b=b: h.dma_start(
                out=g2_out[oc * 128:(oc + 1) * 128, t0:t0 + W], in_=g2_i[b][:, :]),
                reads=[("g2i", b)], sem_key="g2o%d" % b)
        fw.fence()
        cross_partition_max(fw, cons, mx1[:, 0:1], tmp128, rowb, pbk[:, 0:128],
                            mrow[0:1, 0:1], [("cp", 0), ("cp", 1), ("cp", 2), ("cp", 3), "mx1"])
        tk = fw.dma("sp", lambda h: h.dma_start(out=mo_out, in_=mrow[0:1, 0:1]),
                    reads=[("cp", 3)], sem_key="m_o")
        fw.barrier_wait("sp", [tk])
        fw.finalize(block)
    return nc



def build_f4():
    nc = bass.Bass("TRN2", target_bir_lowering=False)
    x_d = nc.dram_tensor("x", [NTOK, D], F32, kind="ExternalInput").ap()
    s_d = nc.dram_tensor("s_f", [D, NTOK], F32, kind="ExternalInput").ap()
    u_d = nc.dram_tensor("U", [INTER_F, NTOK], I16, kind="ExternalInput").ap()
    g2_d = nc.dram_tensor("G2", [INTER_F, NTOK], I16, kind="ExternalInput").ap()
    wd_d = nc.dram_tensor("w_down", [D, INTER_F], F32, kind="ExternalInput").ap()
    mx8_d = nc.dram_tensor("mx8", [1, 8], F32, kind="ExternalInput").ap()
    mpu8_d = nc.dram_tensor("mpu8", [1, 8], F32, kind="ExternalInput").ap()
    mpg8_d = nc.dram_tensor("mpg8", [1, 8], F32, kind="ExternalInput").ap()
    mp28_d = nc.dram_tensor("mp28", [1, 8], F32, kind="ExternalInput").ap()
    mo8_d = nc.dram_tensor("mo8", [1, 8], F32, kind="ExternalInput").ap()
    y_out = nc.dram_tensor("y", [NTOK, D], F32, kind="ExternalOutput").ap()
    sc_d = nc.dram_tensor("scb", [1, 16], F32).ap()

    es = ExitStack()
    sb = lambda n, s, dt=F32: es.enter_context(nc.sbuf_tensor(n, s, dt))
    ps = lambda n, s, dt=F32: es.enter_context(nc.psum_tensor(n, s, dt))
    with es:
        W = 512
        wdt = sb("wdt", [128, 24 * 1024], BF16)
        o_bf = sb("o_bf", [128, 24 * 1024], BF16)
        u_t = [sb("u_t%d" % i, [128, 1024], I16) for i in range(2)]
        g2_t = [sb("g2_t%d" % i, [128, 1024], I16) for i in range(2)]
        mgc = sb("mgc", [128, 1])
        AF4 = sb("AF4", [128, 12288])
        af4 = lambda a, l: AF4[:, a:a + l]
        qt = af4(0, 3072)
        stage = af4(3072, 3072)
        gq = [af4(6144 + i * 1024, 1024) for i in range(2)]
        uq = [af4(8192 + i * 1024, 1024) for i in range(2)]
        orw = [af4(10240 + i * 1024, 1024) for i in range(2)]
        orw2 = [af4(0 + i * 1024, 1024) for i in range(2)]
        o_i = [sb("o_i_%d" % i, [128, 1024], I32) for i in range(2)]
        blk = sb("blk", [128, 96])
        rblk = sb("rblk", [128, 96])
        gi = sb("gi", [128, 3072], I32)
        wdq = sb("wdq", [128, 3072], BF16)
        g_f = sb("g_f", [128, 512])
        s_t = sb("s_t", [128, 512])
        sg_f = sb("sg_f", [128, 8 * 512])
        x_tm = sb("x_tm", [128, 1024])
        out_tm = sb("out_tm", [128, 1024])
        scal = sb("scal", [1, 16])
        scal_pp = sb("scal_pp", [128, 16])
        m8 = sb("m8", [1, 40])
        cons = {"ident_f": sb("ident_f", [128, 128]),
                "ident_b": sb("ident_b", [128, 128], BF16)}
        c7 = sb("c7", [1, 1])
        pbk = [ps("pbk%d" % i, [128, 512]) for i in range(4)]
        pTb = ps("pTb", [128, 1024], BF16)
        pTf = ps("pTf", [128, 512])
        fw = FW(nc, es)
        block = es.enter_context(nc.Block())
        emit_consts(fw, cons)
        fw.op("dve", lambda h: h.memset(scal[:], 0.0),
              writes=[("sc", i) for i in range(16)])
        fw.op("dve", lambda h: h.memset(mgc[:, 0:1], MAGIC), writes=["mgc"])

        for i, d in enumerate((mx8_d, mpu8_d, mpg8_d, mp28_d, mo8_d)):
            fw.dma("sp", lambda h, i=i, d=d: h.dma_start(out=m8[0:1, i * 8:(i + 1) * 8],
                                                         in_=d),
                   writes=[("m8", i)], sem_key="m8_%d" % i)
        fw.op("dve", lambda h: h.memset(c7[:], QMAX), writes=["c7"])
        fw.op("dve", lambda h: h.reciprocal(scal[0:1, 0:1], c7[0:1, 0:1]),
              reads=["c7"], writes=[("sc", 0)])
        for i, col in ((0, 1), (1, 2), (2, 3), (3, 4), (4, 12)):
            fw.op("dve", lambda h, i=i, col=col: h.reduce_max(
                scal[0:1, col:col + 1], m8[0:1, i * 8:(i + 1) * 8], axis=AX.X),
                reads=[("m8", i)], writes=[("sc", col)])

        def scmul(dst, a, b_):
            fw.op("dve", lambda h: h.tensor_tensor(scal[0:1, dst:dst + 1],
                                                   scal[0:1, a:a + 1],
                                                   scal[0:1, b_:b_ + 1], op=ALU.mult),
                  reads=[("sc", a), ("sc", b_)], writes=[("sc", dst)])

        def scmax12(col):
            fw.op("dve", lambda h: h.tensor_scalar(scal[0:1, col:col + 1],
                                                   scal[0:1, col:col + 1], 1e-12,
                                                   None, ALU.max),
                  reads=[("sc", col)], writes=[("sc", col)])

        def screcip(dst, a):
            fw.op("dve", lambda h: h.reciprocal(scal[0:1, dst:dst + 1],
                                                scal[0:1, a:a + 1]),
                  reads=[("sc", a)], writes=[("sc", dst)])

        scmul(5, 1, 0); scmax12(5)
        scmul(6, 5, 2); scmul(6, 6, 0); scmax12(6)
        scmul(9, 5, 3); scmul(9, 9, 0); scmax12(9)
        fw.op("dve", lambda h: h.tensor_scalar(scal[0:1, 10:11], scal[0:1, 9:10],
                                               1.0 / 65536.0, None, ALU.mult),
              reads=[("sc", 9)], writes=[("sc", 10)])
        scmul(10, 10, 4); scmul(10, 10, 0); scmax12(10)
        scmul(13, 12, 0); scmax12(13)
        screcip(14, 13)
        tb = fw.dma("sp", lambda h: h.dma_start(out=sc_d, in_=scal[0:1, 0:16]),
                    reads=[("sc", i) for i in range(16)],
                    writes=["scb"], sem_key="scb")
        fw.dma("sp", lambda h: h.dma_start(out=scal_pp[:, 0:16],
                                           in_=sc_d.partition_broadcast(128)),
               reads=["scb"], writes=["scal_pp"], sem_key="scpp")
        fw.op("dve", lambda h: h.memset(c7[:], 7.0), reads=[("sc", 0)],
              writes=["c7"])
        fw.op("dve", lambda h: h.reciprocal(scal[0:1, 8:9], c7[0:1, 0:1]),
              reads=["c7"], writes=[("sc", 8)])
        t2 = fw.dma("sp", lambda h: h.dma_start(out=sc_d, in_=scal[0:1, 0:16]),
                    reads=[("sc", i) for i in range(16)], writes=["scb"], sem_key="scb")
        fw.dma("sp", lambda h: h.dma_start(out=scal_pp[:, 0:16],
                                           in_=sc_d.partition_broadcast(128)),
               reads=["scb"], writes=["scal_pp"], sem_key="scpp")

        for rt in range(D // 128):
            fw.dma("sp", lambda h, rt=rt: h.dma_start(
                out=stage[:, :], in_=wd_d[rt * 128:(rt + 1) * 128, :]),
                writes=["stage"], sem_key="w_in")
            fw.op("dve", lambda h: h.reduce_max(
                blk[:, :, None], stage[:].rearrange("p (b k) -> p b k", b=96),
                axis=AX.X, apply_absolute_value=True),
                reads=["stage"], writes=["blk"])
            fw.op("dve", lambda h: h.tensor_scalar(blk[:], blk[:], scal_pp[:, 8:9],
                                                   1e-12, ALU.mult, ALU.max),
                  reads=["blk", "scal_pp"], writes=["blk"])
            fw.op("dve", lambda h: h.reciprocal(rblk[:], blk[:]),
                  reads=["blk"], writes=["rblk"])
            fw.op("dve", lambda h: h.tensor_tensor(
                qt[:].rearrange("p (b k) -> p b k", b=96),
                stage[:].rearrange("p (b k) -> p b k", b=96),
                rblk[:, :, None].broadcast_to([128, 96, 32]), op=ALU.mult),
                reads=["stage", "rblk"], writes=["qtw"])
            fw.op("dve", lambda h: h.tensor_scalar(qt[:], qt[:], MAGIC, MAGIC,
                                                   ALU.add, ALU.subtract),
                  reads=["qtw"], writes=["qtw"])
            fw.op("dve", lambda h: h.tensor_copy(gi[:], qt[:]),
                  reads=["qtw"], writes=["giw"])
            fw.op("dve", lambda h: h.tensor_tensor(
                wdq[:].rearrange("p (b k) -> p b k", b=96),
                gi[:].rearrange("p (b k) -> p b k", b=96),
                blk[:, :, None].broadcast_to([128, 96, 32]), op=ALU.mult),
                reads=["giw", "blk"], writes=["wdq"])
            for ic in range(24):
                fw.op("pe", lambda h, ic=ic: h.transpose(
                    pTb[:, (ic % 8) * 128:(ic % 8) * 128 + 128],
                    wdq[:, ic * 128:(ic + 1) * 128], cons["ident_b"][:]),
                    reads=["wdq", "ident_b"], writes=["pTbank"])
                fw.op("dve", lambda h, ic=ic, rt=rt: h.tensor_copy(
                    wdt[:, ic * 1024 + rt * 128:ic * 1024 + rt * 128 + 128],
                    pTb[:, (ic % 8) * 128:(ic % 8) * 128 + 128]),
                    reads=["pTbank"], writes=[("wdt", ic, rt)])
        fw.fence()

        WO = 1024
        for tt in range(NTOK // WO):
            t0 = tt * WO
            for ic in range(24):
                b = ic % 2
                fw.dma("sp", lambda h, ic=ic, t0=t0, b=b: h.dma_start(
                    out=u_t[b][:, :], in_=u_d[ic * 128:(ic + 1) * 128, t0:t0 + WO]),
                    writes=[("u_t", b)], sem_key="u_in%d" % b)
                fw.dma("sp", lambda h, ic=ic, t0=t0, b=b: h.dma_start(
                    out=g2_t[b][:, :], in_=g2_d[ic * 128:(ic + 1) * 128, t0:t0 + WO]),
                    writes=[("g2_t", b)], sem_key="g2_in%d" % b)
                fw.op("act", lambda h, b=b: h.activation(gq[b][:, :], g2_t[b][:, :],
                                                         AF.Identity,
                                                         scale=scal_pp[:, 10:11]),
                      reads=[("g2_t", b), "scal_pp"], writes=[("gq2", b)])
                fw.op("dve", lambda h, b=b: h.tensor_scalar(uq[b][:, :], u_t[b][:, :],
                                                            scal_pp[:, 6:7], None,
                                                            ALU.mult),
                      reads=[("u_t", b), "scal_pp"], writes=[("uq", b)])
                fw.op("dve", lambda h, b=b: h.tensor_tensor(orw[b][:, :], gq[b][:, :],
                                                            uq[b][:, :], op=ALU.mult),
                      reads=[("gq2", b), ("uq", b)], writes=[("orw", b)])
                fw.op("act", lambda h, b=b: h.activation(orw2[b][:, :], orw[b][:, :],
                                                         AF.Identity,
                                                         scale=scal_pp[:, 14:15],
                                                         bias=mgc[:, 0:1]),
                      reads=[("orw", b), "scal_pp", "mgc"], writes=[("orw2", b)])
                fw.op("dve", lambda h, b=b: h.tensor_scalar(o_i[b][:, :], orw2[b][:, :],
                                                            MAGIC, None, ALU.subtract),
                      reads=[("orw2", b)], writes=[("o_i", b)])
                fw.op("dve", lambda h, ic=ic, b=b: h.tensor_copy(
                    o_bf[:, ic * WO:(ic + 1) * WO], o_i[b][:, :]),
                    reads=[("o_i", b)], writes=[("ob", ic)])
            for half in range(WO // 512):
                hof = half * 512
                for dc in range(8):
                    bk = pbk[dc % 2]
                    for ic in range(24):
                        fw.op("pe", lambda h, dc=dc, ic=ic, bk=bk, hof=hof: h.matmul(
                            bk[:, 0:512],
                            wdt[:, ic * 1024 + dc * 128:ic * 1024 + dc * 128 + 128],
                            o_bf[:, ic * WO + hof:ic * WO + hof + 512],
                            start=(ic == 0), stop=(ic == 23)),
                            reads=[("wdt", ic, dc), ("ob", ic)],
                            writes=[("pd", dc % 2)], accum=True)
                    fw.op("act", lambda h, bk=bk, dc=dc: h.activation(
                        g_f[:, :], bk[:, 0:512], AF.Silu, scale=scal_pp[:, 13:14]),
                        reads=[("pd", dc % 2), "scal_pp"], writes=["g_f"])
                    fw.dma("sp", lambda h, dc=dc, t0=t0, hof=hof: h.dma_start(
                        out=s_t[:, :], in_=s_d[dc * 128:(dc + 1) * 128,
                                               t0 + hof:t0 + hof + 512]),
                        writes=["s_t"], sem_key="s_in")
                    fw.op("dve", lambda h, dc=dc: h.tensor_tensor(
                        sg_f[:, dc * 512:(dc + 1) * 512], g_f[:, :], s_t[:, :],
                        op=ALU.mult),
                        reads=["g_f", "s_t"], writes=[("sg", dc)])
                for sub in range(4):
                    ts0 = t0 + hof + sub * 128
                    fw.dma("sp", lambda h, ts0=ts0: h.dma_start(
                        out=x_tm[:, :], in_=x_d[ts0:ts0 + 128, :]),
                        writes=["x_tm"], sem_key="x_in")
                    for dc in range(8):
                        fw.op("pe", lambda h, dc=dc, sub=sub: h.transpose(
                            pTf[:, (dc % 4) * 128:(dc % 4) * 128 + 128],
                            sg_f[:, dc * 512 + sub * 128:dc * 512 + sub * 128 + 128],
                            cons["ident_f"][:]),
                            reads=[("sg", dc), "ident_f"], writes=["pTf"])
                        fw.op("dve", lambda h, dc=dc: h.tensor_tensor(
                            out_tm[:, dc * 128:(dc + 1) * 128],
                            pTf[:, (dc % 4) * 128:(dc % 4) * 128 + 128],
                            x_tm[:, dc * 128:(dc + 1) * 128], op=ALU.add),
                            reads=["pTf", "x_tm"], writes=[("ot", dc)])
                    fw.dma("pool", lambda h, ts0=ts0: h.dma_start(
                        out=y_out[ts0:ts0 + 128, :], in_=out_tm[:, :]),
                        reads=[("ot", dc2) for dc2 in range(8)], sem_key="y_out")
        fw.barrier_wait("sp", [("d", "y_out", fw.dcnt["y_out"])])
        fw.finalize(block)
    return nc



_CACHE = {}


def _get(name, builder):
    if name not in _CACHE:
        _CACHE[name] = builder()
    return _CACHE[name]


def kernel(L0_attn_hidden_states, L0_attn_attn_norm_w, L0_attn_Wq, L0_attn_Wk,
           L0_attn_Wv, L0_attn_Wo, L0_attn_attn_out_norm_w, L0_attn_mlp_norm_w,
           L0_attn_Wgate, L0_attn_Wup, L0_attn_Wdown, L0_ffn_w_gate,
           L0_ffn_w_up, L0_ffn_w_down):
    x = np.ascontiguousarray(np.asarray(L0_attn_hidden_states, dtype=np.float32))
    xs = [np.ascontiguousarray(x[2 * i:2 * i + 2].reshape(NTOK, D))
          for i in range(CORES)]
    f32 = lambda a: np.ascontiguousarray(np.asarray(a, dtype=np.float32))
    cores = list(range(CORES))

    nc1 = _get("l1", build_l1)
    common1 = dict(wq=f32(L0_attn_Wq), wk=f32(L0_attn_Wk), wv=f32(L0_attn_Wv),
                   wo=f32(L0_attn_Wo), wgate=f32(L0_attn_Wgate),
                   wup=f32(L0_attn_Wup), wdown=f32(L0_attn_Wdown),
                   attn_norm_w=f32(L0_attn_attn_norm_w),
                   attn_out_norm_w=f32(L0_attn_attn_out_norm_w),
                   mlp_norm_w=f32(L0_attn_mlp_norm_w))
    r1 = run_bass_kernel_spmd(nc1, [dict(x=xs[i], **common1) for i in cores],
                              core_ids=cores)
    s_f = [r1.results[i]["s_f"] for i in cores]
    mx8 = np.ascontiguousarray(
        np.concatenate([r1.results[i]["m_x"] for i in cores], axis=1))

    nc3 = _get("f1", build_f1)
    common3 = dict(mx8=mx8, w_up=f32(L0_ffn_w_up), w_gate=f32(L0_ffn_w_gate))
    r3 = run_bass_kernel_spmd(nc3, [dict(x=xs[i], **common3) for i in cores],
                              core_ids=cores)
    P_up = [r3.results[i]["P_up"] for i in cores]
    P_gate = [r3.results[i]["P_gate"] for i in cores]
    mpu8 = np.ascontiguousarray(
        np.concatenate([r3.results[i]["m_pu"] for i in cores], axis=1))
    mpg8 = np.ascontiguousarray(
        np.concatenate([r3.results[i]["m_pg"] for i in cores], axis=1))

    nc4 = _get("f2", build_f2)
    r4 = run_bass_kernel_spmd(
        nc4, [dict(P_gate=P_gate[i], mx8=mx8, mpg8=mpg8) for i in cores],
        core_ids=cores)
    G = [r4.results[i]["G"] for i in cores]
    S = [r4.results[i]["S"] for i in cores]
    mp28 = np.ascontiguousarray(
        np.concatenate([r4.results[i]["m_p2"] for i in cores], axis=1))

    nc5 = _get("f3", build_f3)
    r5 = run_bass_kernel_spmd(
        nc5, [dict(P_up=P_up[i], G=G[i], S=S[i], mx8=mx8, mpu8=mpu8,
                   mpg8=mpg8, mp28=mp28) for i in cores],
        core_ids=cores)
    U = [r5.results[i]["U"] for i in cores]
    G2 = [r5.results[i]["G2"] for i in cores]
    mo8 = np.ascontiguousarray(
        np.concatenate([r5.results[i]["m_o"] for i in cores], axis=1))

    nc6 = _get("f4", build_f4)
    common6 = dict(w_down=f32(L0_ffn_w_down), mx8=mx8, mpu8=mpu8, mpg8=mpg8,
                   mp28=mp28, mo8=mo8)
    r6 = run_bass_kernel_spmd(
        nc6, [dict(x=xs[i], s_f=s_f[i], U=U[i], G2=G2[i], **common6)
              for i in cores],
        core_ids=cores)
    out = np.concatenate([r6.results[i]["y"].reshape(2, T, D) for i in cores],
                         axis=0)
    return out
